# version 03
# baseline (speedup 1.0000x reference)
import numpy as np
from contextlib import ExitStack

import concourse.bass as bass
import concourse.mybir as mybir
from concourse.bass_utils import run_bass_kernel_spmd

F32 = mybir.dt.float32
BF16 = mybir.dt.bfloat16
AF = mybir.ActivationFunctionType
ALU = mybir.AluOpType
AX = mybir.AxisListType.X

D = 1024
T = 2048
NCH = 8
TS = 512
EPS = 1e-5
RG = [[0, 1], [2, 3], [4, 5], [6, 7]]


def build():
    nc = bass.Bass("TRN2", target_bir_lowering=False)
    XT = nc.dram_tensor("xT", [D, T], F32, kind="ExternalInput")
    WT = nc.dram_tensor("wT", [D, D], F32, kind="ExternalInput")
    WV = nc.dram_tensor("wv", [D, D], F32, kind="ExternalInput")
    WO = nc.dram_tensor("wo", [D, D], F32, kind="ExternalInput")
    BV = nc.dram_tensor("bv", [1, D], F32, kind="ExternalInput")
    BO = nc.dram_tensor("bo", [1, D], F32, kind="ExternalInput")
    YT = nc.dram_tensor("yT", [D, T], F32, kind="ExternalOutput")
    cc_in = nc.dram_tensor("cc_in", [128, 8], F32)
    cc_out = nc.dram_tensor("cc_out", [128, 8], F32)

    es = ExitStack()
    with es:
        def sbt(name, shape, dt):
            return es.enter_context(nc.sbuf_tensor(name, shape, dt))

        x = [sbt(f"x{i}", [128, T], F32) for i in range(NCH)]
        xb = [sbt(f"xb{i}", [128, T], BF16) for i in range(NCH)]
        wt = sbt("wt", [128, 8 * D], BF16)
        wvb = sbt("wvb", [128, 8 * D], BF16)
        wob = sbt("wob", [128, 8 * D], BF16)
        scrA = sbt("scrA", [128, 2 * D], F32)
        sq = [sbt("sqa", [128, T], BF16), sbt("sqb", [128, T], BF16)]
        rb = sbt("rb", [128, T], F32)
        yt = sbt("yt", [128, TS], F32)
        msq = sbt("msq", [1, T], F32)
        veps = sbt("veps", [1, T], F32)
        rstd = sbt("rstd", [1, T], F32)
        mbf = sbt("mbf", [1, T], BF16)
        wsneg = sbt("wsneg", [1, D], BF16)
        mvorow = sbt("mvorow", [1, D], F32)
        absp = sbt("absp", [128, 16], F32)
        abs1 = sbt("abs1", [128, 1], F32)
        tiny = sbt("tiny", [1, 16], F32)
        thb = sbt("thb", [128, 1], F32)
        thnb = sbt("thnb", [128, 1], F32)
        taub = sbt("taub", [128, 1], F32)
        u = sbt("u", [128, 8], F32)
        ubf = sbt("ubf", [128, 8], BF16)
        mvt = sbt("mvt", [128, 8], F32)
        mvtb = sbt("mvtb", [128, 8], BF16)
        cfm = sbt("cfm", [128, 8], F32)
        ones_b = sbt("ones_b", [128, 1], BF16)
        ones_f = sbt("ones_f", [128, 1], F32)
        onesr = sbt("onesr", [1, 128], F32)
        id1 = sbt("id1", [1, 1], F32)

        pb = [es.enter_context(nc.psum_tensor(f"pb{i}", [128, 512], F32))
              for i in range(8)]

        sems = {n: es.enter_context(nc.semaphore(n))
                for n in ("sL", "sO", "sC", "sP", "sV", "sA")}
        block = es.enter_context(nc.Block())

        ops = {q: [] for q in ("sync", "tensor", "vector", "scalar", "gpsimd")}
        cnt = {n: 0 for n in sems}
        waited = {q: {} for q in ops}

        def emit(q, fn):
            ops[q].append(fn)

        def wait(q, ev):
            if ev is None:
                return
            s, v = ev
            if waited[q].get(s, 0) >= v:
                return
            waited[q][s] = v
            emit(q, lambda e, _s=sems[s], _v=v: e.wait_ge(_s, _v))

        def inc(s, n=1):
            cnt[s] += n
            return (s, cnt[s])

        def I(q, meth, *a, sem=None, n=1, **k):
            def fn(e, _a=a, _k=k, _m=meth, _sem=sem, _n=n):
                r = getattr(e, _m)(*_a, **_k)
                if _sem is not None:
                    r.then_inc(sems[_sem], _n)
            emit(q, fn)
            return inc(sem, n) if sem is not None else None

        bank_free = [None] * 8
        sq_guard = [None, None]
        xb_free = [None]

        def slot(k):
            return scrA[:, (k % 2) * D:(k % 2 + 1) * D]

        I("vector", "memset", ones_b[:, :], 1.0)
        I("vector", "memset", ones_f[:, :], 1.0)
        I("vector", "memset", onesr[:, :], 1.0)
        ev_const = I("vector", "memset", id1[:, :], 1.0, sem="sV")

        ev_x_dma = [None] * NCH
        ev_absr = [None] * NCH
        xi = wi = 0
        for kind in (["x", "w1"] * 8)[:16]:
            if kind == "x":
                k = xi; xi += 1
                ev_x_dma[k] = I("sync", "dma_start", out=x[k][:, :],
                                in_=XT[k * 128:(k + 1) * 128, :], sem="sL", n=16)
            else:
                k = wi; wi += 1
                if k >= 2:
                    wait("sync", ev_absr[k - 2])
                ev_w1 = I("sync", "dma_start", out=slot(k),
                          in_=WT[k * 128:(k + 1) * 128, :], sem="sL", n=16)
                wait("vector", ev_w1)
                sl = slot(k)
                I("vector", "tensor_reduce", out=absp[:, 2 * k:2 * k + 1],
                  in_=sl[:, 0:512], axis=AX, op=ALU.add,
                  apply_absolute_value=True)
                ev_absr[k] = I("vector", "tensor_reduce",
                               out=absp[:, 2 * k + 1:2 * k + 2],
                               in_=sl[:, 512:1024], axis=AX, op=ALU.add,
                               apply_absolute_value=True, sem="sV")

        wait("vector", ev_absr[NCH - 1])
        ev_abs1 = I("vector", "tensor_reduce", out=abs1[:, :], in_=absp[:, :],
                    axis=AX, op=ALU.add, sem="sV")
        wait("vector", ev_abs1)
        wait("tensor", ev_const)
        wait("tensor", ev_abs1)
        ev_sumabs = I("tensor", "matmul", pb[7][0:1, 0:1], ones_f[:, :], abs1[:, :],
                      start=True, stop=True, sem="sP")
        wait("vector", ev_sumabs)
        ev_scale = I("vector", "tensor_scalar", out=tiny[0:1, 1:2], in0=pb[7][0:1, 0:1],
                     scalar1=1.0 / float(D * D), scalar2=EPS,
                     op0=ALU.mult, op1=ALU.max, sem="sV")
        wait("vector", ev_scale)
        ev_scomb = I("vector", "tensor_scalar", out=tiny[0:1, 3:4], in0=tiny[0:1, 1:2],
                     scalar1=0.5, scalar2=0.0, op0=ALU.mult, op1=ALU.add, sem="sV")
        bank_free[7] = ev_scale
        wait("tensor", ev_scomb)
        ev_thb_mm = I("tensor", "matmul", pb[6][:, 0:1], onesr[:, :], tiny[0:1, 3:4],
                      start=True, stop=True, sem="sP")
        wait("vector", ev_thb_mm)
        ev_thb = I("vector", "tensor_copy", out=thb[:, :], in_=pb[6][:, 0:1],
                   sem="sV")
        wait("vector", ev_thb)
        ev_thnb = I("vector", "tensor_scalar", out=thnb[:, :], in0=thb[:, :],
                    scalar1=-1.0, scalar2=0.0, op0=ALU.mult, op1=ALU.add, sem="sV")
        bank_free[6] = ev_thb

        ev_wts = [None] * NCH
        ev_wt = [None] * NCH
        for k in range(NCH):
            if k >= 2:
                wait("sync", ev_wts[k - 2])
            ev_w2 = I("sync", "dma_start", out=slot(k),
                      in_=WT[k * 128:(k + 1) * 128, :], sem="sL", n=16)
            wait("vector", ev_w2)
            wait("vector", ev_thnb)
            h = (k % 2) * D
            I("vector", "tensor_scalar", out=sq[0][:, h:h + D], in0=slot(k),
              scalar1=thb[:, 0:1], scalar2=0.0, op0=ALU.is_ge, op1=ALU.add)
            ev_wts[k] = I("vector", "tensor_scalar", out=sq[1][:, h:h + D],
                          in0=slot(k), scalar1=thnb[:, 0:1], scalar2=0.0,
                          op0=ALU.is_le, op1=ALU.add, sem="sV")
            wait("vector", ev_wts[k])
            ev_wt[k] = I("vector", "tensor_tensor", out=wt[:, k * D:(k + 1) * D],
                         in0=sq[0][:, h:h + D], in1=sq[1][:, h:h + D],
                         op=ALU.subtract, sem="sV")

        wv_dma = [None] * NCH
        wo_dma = [None] * NCH
        ev_wvb = [None] * NCH
        ev_wob = [None] * NCH

        def ln_stats(x_ready, kind):
            ev_cast = [None] * NCH
            ev_s2pass = [None] * NCH
            for kc in range(NCH):
                wait("scalar", x_ready[kc])
                wait("scalar", xb_free[0])
                ev_cast[kc] = I("scalar", "copy", out=xb[kc][:, :], in_=x[kc][:, :],
                                sem="sA")
                wait("vector", ev_cast[kc])
                wait("vector", ev_s2pass[kc - 2] if kc >= 2 else sq_guard[kc % 2])
                ev_sq = I("vector", "tensor_tensor", out=sq[kc % 2][:, :],
                          in0=xb[kc][:, :], in1=xb[kc][:, :], op=ALU.mult, sem="sV")
                wait("tensor", ev_cast[kc])
                if kc == 0:
                    for s in range(4):
                        wait("tensor", bank_free[s])
                for s in range(4):
                    I("tensor", "matmul", pb[s][0:1, :], ones_b[:, :],
                      xb[kc][:, s * TS:(s + 1) * TS],
                      start=(kc == 0), stop=(kc == NCH - 1))
                wait("tensor", ev_sq)
                if kc == 0:
                    for s in range(4):
                        wait("tensor", bank_free[4 + s])
                for s in range(4):
                    sem = "sP" if s == 3 else None
                    ev = I("tensor", "matmul", pb[4 + s][0:1, :], ones_b[:, :],
                           sq[kc % 2][:, s * TS:(s + 1) * TS],
                           start=(kc == 0), stop=(kc == NCH - 1), sem=sem)
                    if s == 3:
                        ev_s2pass[kc] = ev
            stats_done = ev_s2pass[NCH - 1]
            sq_guard[0] = ev_s2pass[NCH - 2]
            sq_guard[1] = ev_s2pass[NCH - 1]
            xb_free[0] = stats_done

            ev_msq = [None] * 4
            ev_veps = [None] * 4
            ev_m = [None] * 4
            for s in range(4):
                wait("scalar", stats_done)
                ev_msq[s] = I("scalar", "activation", out=msq[0:1, s * TS:(s + 1) * TS],
                              in_=pb[s][0:1, :], func=AF.Square, bias=0.0,
                              scale=1.0 / float(D), sem="sA")
            for s in range(4):
                ev_veps[s] = I("scalar", "activation", out=veps[0:1, s * TS:(s + 1) * TS],
                               in_=pb[4 + s][0:1, :], func=AF.Copy, bias=EPS,
                               scale=1.0 / float(D), sem="sA")
            if kind == "ffn":
                for s in range(4):
                    ev_m[s] = I("scalar", "activation",
                                out=mbf[0:1, s * TS:(s + 1) * TS],
                                in_=pb[s][0:1, :], func=AF.Copy, bias=0.0,
                                scale=1.0 / float(D), sem="sA")
            ev_v2 = [None] * 4
            for s in range(4):
                wait("vector", ev_veps[s])
                wait("vector", ev_msq[s])
                ev_v2[s] = I("vector", "tensor_tensor",
                             out=veps[0:1, s * TS:(s + 1) * TS],
                             in0=veps[0:1, s * TS:(s + 1) * TS],
                             in1=msq[0:1, s * TS:(s + 1) * TS], op=ALU.subtract,
                             sem="sV")
            ev_ln = [None] * 4
            for s in range(4):
                wait("scalar", ev_v2[s])
                ev_ln[s] = I("scalar", "activation", out=msq[0:1, s * TS:(s + 1) * TS],
                             in_=veps[0:1, s * TS:(s + 1) * TS], func=AF.Ln,
                             bias=0.0, scale=1.0, sem="sA")
            ev_sqrt = [None] * 4
            wait("scalar", ev_ln[3])
            for s in range(4):
                ev_sqrt[s] = I("scalar", "activation", out=rstd[0:1, s * TS:(s + 1) * TS],
                               in_=msq[0:1, s * TS:(s + 1) * TS], func=AF.Exp,
                               bias=0.0, scale=-0.5, sem="sA")
            ev_rb = [None] * 4
            tau_evs = []
            for s in range(4):
                wait("vector", ev_sqrt[s])
                ev_rstd = ev_sqrt[s]
                ev_tau = None
                if kind == "ffn":
                    wait("vector", ev_scomb)
                    ev_rstd = I("vector", "tensor_scalar",
                                out=rstd[0:1, s * TS:(s + 1) * TS],
                                in0=rstd[0:1, s * TS:(s + 1) * TS],
                                scalar1=tiny[0:1, 3:4], scalar2=0.0,
                                op0=ALU.mult, op1=ALU.add, sem="sV")
                else:
                    ev_tp = I("vector", "tensor_tensor",
                              out=veps[0:1, s * TS:(s + 1) * TS],
                              in0=pb[s][0:1, :], in1=rstd[0:1, s * TS:(s + 1) * TS],
                              op=ALU.mult, sem="sV")
                    wait("vector", ev_tp)
                    tau_evs.append(None)
                    ev_tau = I("vector", "tensor_reduce",
                               out=tiny[0:1, 8 + s:9 + s],
                               in_=veps[0:1, s * TS:(s + 1) * TS],
                               axis=AX, op=ALU.add, sem="sV")
                    tau_evs[-1] = ev_tau
                wait("tensor", ev_msq[s])
                if kind == "ffn":
                    wait("tensor", ev_m[s])
                if ev_tau is not None:
                    wait("tensor", ev_tau)
                wait("tensor", ev_rstd)
                ev_bc = I("tensor", "matmul", pb[s][:, :], onesr[:, :],
                          rstd[0:1, s * TS:(s + 1) * TS], start=True, stop=True,
                          sem="sP")
                wait("vector", ev_bc)
                ev_rb[s] = I("vector", "tensor_copy", out=rb[:, s * TS:(s + 1) * TS],
                             in_=pb[s][:, 0:512], sem="sV")
                bank_free[s] = ev_rb[s]
                bank_free[4 + s] = ev_veps[s]
            return ev_rb, (ev_m if kind == "ffn" else None), stats_done, tau_evs

        def ffn_body(ev_rb, ev_m, ev_wsneg):
            x_done = [None] * NCH
            last_grp = None
            for oc in range(NCH):
                bset = [4 * (oc % 2) + s for s in range(4)]
                for s, b in enumerate(bset):
                    wait("tensor", bank_free[b])
                for kc in range(NCH):
                    wait("tensor", ev_wt[kc])
                    for s, b in enumerate(bset):
                        I("tensor", "matmul", pb[b][:, :],
                          wt[:, kc * D + oc * 128:kc * D + (oc + 1) * 128],
                          xb[kc][:, s * TS:(s + 1) * TS],
                          start=(kc == 0), stop=False)
                wait("tensor", ev_wsneg[0])
                wait("tensor", ev_wsneg[1])
                ev_grp = [None] * 4
                for s, b in enumerate(bset):
                    wait("tensor", ev_m[s])
                    ev_grp[s] = I("tensor", "matmul", pb[b][:, :],
                                  wsneg[0:1, oc * 128:(oc + 1) * 128],
                                  mbf[0:1, s * TS:(s + 1) * TS],
                                  start=False, stop=True, sem="sP")
                last_grp = ev_grp[3]
                adds = []
                for s, b in enumerate(bset):
                    wait("vector", ev_grp[s])
                    wait("vector", ev_rb[s])
                    ev_mul = I("vector", "tensor_tensor", out=yt[:, :],
                               in0=pb[b][:, :], in1=rb[:, s * TS:(s + 1) * TS],
                               op=ALU.mult, sem="sV")
                    bank_free[b] = ev_mul
                    wait("vector", ev_mul)
                    ev_add = I("vector", "tensor_tensor",
                               out=x[oc][:, s * TS:(s + 1) * TS],
                               in0=x[oc][:, s * TS:(s + 1) * TS],
                               in1=yt[:, :], op=ALU.add, sem="sV")
                    adds.append(ev_add)
                x_done[oc] = adds[3]
            xb_free[0] = last_grp
            return x_done

        ev_rb1, ev_m1, stats1, _ = ln_stats(ev_x_dma, "ffn")

        wait("tensor", bank_free[4])
        wait("tensor", bank_free[5])
        evw = None
        for kc in range(NCH):
            wait("tensor", ev_wt[kc])
            I("tensor", "matmul", pb[4][0:1, :], ones_b[:, :],
              wt[:, kc * D:kc * D + 512], start=(kc == 0), stop=(kc == NCH - 1))
            sem = "sP" if kc == NCH - 1 else None
            evw = I("tensor", "matmul", pb[5][0:1, :], ones_b[:, :],
                    wt[:, kc * D + 512:(kc + 1) * D], start=(kc == 0),
                    stop=(kc == NCH - 1), sem=sem)
        ev_wsneg = [None, None]
        for hh in range(2):
            wait("vector", evw)
            ev_wsneg[hh] = I("vector", "tensor_scalar",
                             out=wsneg[0:1, hh * 512:(hh + 1) * 512],
                             in0=pb[4 + hh][0:1, :], scalar1=-1.0, scalar2=0.0,
                             op0=ALU.mult, op1=ALU.add, sem="sV")
            bank_free[4 + hh] = ev_wsneg[hh]

        x_done1 = ffn_body(ev_rb1, ev_m1, ev_wsneg)

        for k in range(NCH):
            if k < 2:
                wait("sync", ev_wts[6 + k])
            else:
                wait("sync", ev_wvb[k - 2])
            wv_dma[k] = I("sync", "dma_start", out=slot(k),
                          in_=WV[k * 128:(k + 1) * 128, :], sem="sL", n=16)
            wait("scalar", wv_dma[k])
            ev_wvb[k] = I("scalar", "copy", out=wvb[:, k * D:(k + 1) * D],
                          in_=slot(k), sem="sA")
        for k in range(NCH):
            if k < 2:
                wait("sync", ev_wvb[6 + k])
            else:
                wait("sync", ev_wob[k - 2])
            wo_dma[k] = I("sync", "dma_start", out=slot(k),
                          in_=WO[k * 128:(k + 1) * 128, :], sem="sL", n=16)
            wait("scalar", wo_dma[k])
            ev_wob[k] = I("scalar", "copy", out=wob[:, k * D:(k + 1) * D],
                          in_=slot(k), sem="sA")
        bvr = scrA[0:1, 0:D]
        bor = scrA[0:1, D:2 * D]
        wait("sync", ev_wob[6])
        ev_bv = I("sync", "dma_start", out=bvr, in_=BV[:, :], sem="sL", n=16)
        wait("sync", ev_wob[7])
        ev_bo = I("sync", "dma_start", out=bor, in_=BO[:, :], sem="sL", n=16)

        ev_rb2, _, stats2, tau_evs2 = ln_stats(x_done1, "attn")

        for ic in range(NCH):
            evp = None
            for s in range(4):
                wait("vector", ev_rb2[s])
                evm = I("vector", "tensor_tensor", out=yt[:, :],
                        in0=x[ic][:, s * TS:(s + 1) * TS],
                        in1=rb[:, s * TS:(s + 1) * TS], op=ALU.mult, sem="sV")
                wait("vector", evm)
                evp = I("vector", "tensor_reduce", out=absp[:, s:s + 1],
                        in_=yt[:, :], axis=AX, op=ALU.add, sem="sV")
            wait("vector", evp)
            I("vector", "tensor_reduce", out=u[:, ic:ic + 1], in_=absp[:, 0:4],
              axis=AX, op=ALU.add)
        for ev in tau_evs2:
            wait("vector", ev)
        ev_tred = I("vector", "tensor_reduce", out=tiny[0:1, 4:5], in_=tiny[0:1, 8:12],
                    axis=AX, op=ALU.add, sem="sV")
        wait("vector", ev_tred)
        ev_tau = I("vector", "tensor_scalar", out=tiny[0:1, 4:5], in0=tiny[0:1, 4:5],
                   scalar1=1.0 / float(D), scalar2=0.0, op0=ALU.mult, op1=ALU.add,
                   sem="sV")
        wait("tensor", bank_free[2])
        wait("tensor", ev_tau)
        ev_taub_mm = I("tensor", "matmul", pb[2][:, 0:1], onesr[:, :],
                       tiny[0:1, 4:5], start=True, stop=True, sem="sP")
        wait("vector", ev_taub_mm)
        ev_taub = I("vector", "tensor_copy", out=taub[:, :], in_=pb[2][:, 0:1],
                    sem="sV")
        bank_free[2] = ev_taub
        wait("vector", ev_taub)
        ev_un = I("vector", "tensor_scalar", out=u[:, :], in0=u[:, :],
                  scalar1=taub[:, 0:1], scalar2=1.0 / 4096.0,
                  op0=ALU.subtract, op1=ALU.mult, sem="sV")

        wait("gpsimd", ev_un)
        ev_ccs = I("gpsimd", "dma_start", out=cc_in[:, :], in_=u[:, :],
                   sem="sC", n=16)
        wait("gpsimd", ev_ccs)
        ev_cc = I("gpsimd", "collective_compute", "AllReduce", ALU.add,
                  replica_groups=RG, ins=[cc_in.ap().opt()],
                  outs=[cc_out.ap().opt()], sem="sC", n=1)
        wait("gpsimd", ev_cc)
        ev_ccl = I("gpsimd", "dma_start", out=u[:, :], in_=cc_out[:, :],
                   sem="sC", n=16)
        wait("vector", ev_ccl)
        ev_ubf = I("vector", "tensor_copy", out=ubf[:, :], in_=u[:, :], sem="sV")

        wait("tensor", bank_free[0])
        wait("tensor", bank_free[1])
        wait("tensor", ev_ubf)
        for k in range(NCH):
            wait("tensor", ev_wvb[k])
        ev_mv = [None, None]
        for kc in range(NCH):
            for hh in range(2):
                sem = "sP" if kc == NCH - 1 else None
                ev = I("tensor", "matmul", pb[hh][0:1, :], ubf[:, kc:kc + 1],
                       wvb[:, kc * D + hh * 512:kc * D + (hh + 1) * 512],
                       start=(kc == 0), stop=(kc == NCH - 1), sem=sem)
                if kc == NCH - 1:
                    ev_mv[hh] = ev
        ev_mvrow = [None, None]
        for hh in range(2):
            wait("vector", ev_mv[hh])
            wait("vector", ev_bv)
            ev_mvrow[hh] = I("vector", "tensor_tensor",
                             out=mvorow[0:1, hh * 512:(hh + 1) * 512],
                             in0=pb[hh][0:1, :],
                             in1=bvr[0:1, hh * 512:(hh + 1) * 512], op=ALU.add,
                             sem="sV")
            bank_free[hh] = ev_mvrow[hh]
        wait("tensor", bank_free[2])
        ev_tr1 = None
        for c in range(NCH):
            wait("tensor", ev_mvrow[c // 4])
            sem = "sP" if c == NCH - 1 else None
            ev_tr1 = I("tensor", "transpose", pb[2][:, c:c + 1],
                       mvorow[0:1, c * 128:(c + 1) * 128], id1[:, :], sem=sem) or ev_tr1
        wait("vector", ev_tr1)
        ev_mvt = I("vector", "tensor_copy", out=mvt[:, :], in_=pb[2][:, 0:8],
                   sem="sV")
        wait("vector", ev_mvt)
        ev_mvtb = I("vector", "tensor_copy", out=mvtb[:, :], in_=mvt[:, :],
                    sem="sV")
        bank_free[2] = ev_mvtb
        wait("tensor", bank_free[0])
        wait("tensor", bank_free[1])
        wait("tensor", ev_mvtb)
        for k in range(NCH):
            wait("tensor", ev_wob[k])
        ev_or = [None, None]
        for kc in range(NCH):
            for hh in range(2):
                sem = "sP" if kc == NCH - 1 else None
                ev = I("tensor", "matmul", pb[hh][0:1, :], mvtb[:, kc:kc + 1],
                       wob[:, kc * D + hh * 512:kc * D + (hh + 1) * 512],
                       start=(kc == 0), stop=(kc == NCH - 1), sem=sem)
                if kc == NCH - 1:
                    ev_or[hh] = ev
        ev_orow = [None, None]
        for hh in range(2):
            wait("vector", ev_or[hh])
            wait("vector", ev_bo)
            wait("vector", ev_tr1)
            ev_orow[hh] = I("vector", "tensor_tensor",
                            out=mvorow[0:1, hh * 512:(hh + 1) * 512],
                            in0=pb[hh][0:1, :],
                            in1=bor[0:1, hh * 512:(hh + 1) * 512], op=ALU.add,
                            sem="sV")
            bank_free[hh] = ev_orow[hh]
        wait("tensor", bank_free[2])
        ev_tr2 = None
        for c in range(NCH):
            wait("tensor", ev_orow[c // 4])
            sem = "sP" if c == NCH - 1 else None
            ev_tr2 = I("tensor", "transpose", pb[2][:, c:c + 1],
                       mvorow[0:1, c * 128:(c + 1) * 128], id1[:, :], sem=sem) or ev_tr2
        wait("vector", ev_tr2)
        ev_cfm = I("vector", "tensor_copy", out=cfm[:, :], in_=pb[2][:, 0:8],
                   sem="sV")
        bank_free[2] = ev_cfm
        x_done2 = [None] * NCH
        for ic in range(NCH):
            wait("vector", ev_cfm)
            x_done2[ic] = I("vector", "tensor_scalar", out=x[ic][:, :],
                            in0=x[ic][:, :], scalar1=cfm[:, ic:ic + 1],
                            scalar2=0.0, op0=ALU.add, op1=ALU.add, sem="sV")

        ev_rb3, ev_m3, stats3, _ = ln_stats(x_done2, "ffn")
        x_done3 = ffn_body(ev_rb3, ev_m3, ev_wsneg)

        for oc in range(NCH):
            wait("sync", x_done3[oc])
            I("sync", "dma_start", out=YT[oc * 128:(oc + 1) * 128, :],
              in_=x[oc][:, :], sem="sO", n=16)
        wait("sync", ("sO", 16 * NCH))

        @block.sync
        def _(e):
            for fn in ops["sync"]:
                fn(e)

        @block.tensor
        def _(e):
            for fn in ops["tensor"]:
                fn(e)

        @block.vector
        def _(e):
            for fn in ops["vector"]:
                fn(e)

        @block.scalar
        def _(e):
            for fn in ops["scalar"]:
                fn(e)

        @block.gpsimd
        def _(e):
            for fn in ops["gpsimd"]:
                fn(e)

    return nc


def kernel(**inputs):
    xf = np.asarray(inputs["L0_attn_x"], dtype=np.float32)
    wv = np.ascontiguousarray(np.asarray(inputs["L0_attn_wv"], dtype=np.float32))
    wo = np.ascontiguousarray(np.asarray(inputs["L0_attn_wo"], dtype=np.float32))
    bv = np.asarray(inputs["L0_attn_bv"], dtype=np.float32).reshape(1, D)
    bo = np.asarray(inputs["L0_attn_bo"], dtype=np.float32).reshape(1, D)
    wT = np.ascontiguousarray(np.asarray(inputs["L0_ffn_weight"], dtype=np.float32).T)

    nc = build()
    in_maps = []
    for c in range(8):
        b, h = c // 2, c % 2
        xs = np.ascontiguousarray(xf[b, h * T:(h + 1) * T, :].T)
        in_maps.append({"xT": xs, "wT": wT, "wv": wv, "wo": wo,
                        "bv": bv, "bo": bo})
    res = run_bass_kernel_spmd(nc, in_maps, core_ids=list(range(8)))
    out = np.empty((4, 4096, D), dtype=np.float32)
    for c in range(8):
        b, h = c // 2, c % 2
        out[b, h * T:(h + 1) * T, :] = np.asarray(res.results[c]["yT"]).T
    return out
